# version 02
# baseline (speedup 1.0000x reference)
import numpy as np
import ml_dtypes
from contextlib import ExitStack

import concourse.bass as bass
import concourse.mybir as mybir
from concourse.bass_utils import run_bass_kernel_spmd

F32 = mybir.dt.float32
BF16 = mybir.dt.bfloat16
AF = mybir.ActivationFunctionType
ALU = mybir.AluOpType
AX = mybir.AxisListType
BF = ml_dtypes.bfloat16

NCORES = 8
D = 1024
PCH = 8
T = 4096
TC = 512
NTC = T // TC
FF = 4096
NB = 8
EPS_MOD = 1e-5
EPS_LN = 1e-6
EPS_ATTN = 1e-6


class Ctr:

    def __init__(self, sem, name=""):
        self.sem = sem
        self.v = 0
        self.name = name


def soft_barrier(sched, ctrs, engines=("tensor", "scalar", "vector", "gpsimd", "sync")):
    waits = [(c, c.v) for c in ctrs]
    for e in engines:
        sched.op(e, lambda x: None, waits)


class Sched:

    ENGINES = ("sync", "gpsimd", "tensor", "scalar", "vector")

    def __init__(self):
        self.items = {e: [] for e in self.ENGINES}
        self.last_wait = {}

    def op(self, engine, build, waits=()):
        eff = []
        for ctr, val in waits:
            if val <= 0:
                continue
            key = (engine, id(ctr))
            if self.last_wait.get(key, 0) >= val:
                continue
            self.last_wait[key] = val
            eff.append((ctr.sem, val))
        self.items[engine].append((eff, build, None, 0))

    def opi(self, engine, build, ctr, n=1, waits=()):
        eff = []
        for c, val in waits:
            if val <= 0:
                continue
            key = (engine, id(c))
            if self.last_wait.get(key, 0) >= val:
                continue
            self.last_wait[key] = val
            eff.append((c.sem, val))
        self.items[engine].append((eff, build, ctr.sem, n))
        ctr.v += n
        return ctr.v

    def emit(self, block):
        for name in self.ENGINES:
            items = self.items[name]

            def make(items):
                def run(eng):
                    for waits, build, inc_sem, inc_n in items:
                        for sem, val in waits:
                            eng.wait_ge(sem, val)
                        instr = build(eng)
                        if inc_sem is not None:
                            instr.then_inc(inc_sem, inc_n)
                return run
            getattr(block, name)(make(items))


def chunk_ap(sb, pch=None, j0=0, n=TC):
    if pch is None:
        return sb[:, 0:PCH * TC]
    return sb[:, pch * TC + j0: pch * TC + j0 + n]


def bcast8(ap_512):
    return ap_512.rearrange("p (o t) -> p o t", o=1).broadcast_to([128, PCH, TC])


def r3(sb):
    return sb.rearrange("p (o t) -> p o t", t=TC)


class CPool:

    def __init__(self, nc, ctx, values):
        vals = list(dict.fromkeys(values))
        self.sb = ctx.enter_context(nc.sbuf_tensor("cpool", [128, max(1, len(vals))], F32))
        self.idx = {v: i for i, v in enumerate(vals)}

    def emit_init(self, sched):
        for v, i in self.idx.items():
            sched.op("vector", lambda vec, v=v, i=i: vec.memset(self.sb[:, i:i + 1], v))

    def col(self, v):
        i = self.idx[v]
        return self.sb[:, i:i + 1]


class LNState:

    def __init__(self, nc, ctx, pb, scr4=None):
        self.xc = ctx.enter_context(nc.sbuf_tensor("ln_xc", [128, 2 * TC], BF16))
        self.sq = ctx.enter_context(nc.sbuf_tensor("ln_sq", [128, 2 * TC], BF16))
        self.rows = ctx.enter_context(nc.sbuf_tensor("ln_rows", [1, 2 * TC], BF16))
        if scr4 is None:
            scr4 = ctx.enter_context(nc.sbuf_tensor("ln_scr", [128, 4 * TC], F32))
        self.R = scr4[:, 0 * TC:1 * TC]
        self.C = scr4[:, 1 * TC:2 * TC]
        self.u = scr4[:, 2 * TC:3 * TC]
        self.var = scr4[:, 3 * TC:4 * TC]
        self.ones_col = ctx.enter_context(nc.sbuf_tensor("ones_col", [128, 1], BF16))
        self.ones_row = ctx.enter_context(nc.sbuf_tensor("ones_row", [1, 128], BF16))
        self.pb = pb
        self.pe_free = [0, 0]
        self.row_free = 0
        self.uRC_free = 0

    def emit_init(self, sched):
        sched.op("vector", lambda v: v.memset(self.ones_col[:], 1.0))
        sched.op("vector", lambda v: v.memset(self.ones_row[:], 1.0))


def emit_ln(sched, ln, s_sb, eps_col, cT, cS, cV, scale_col=None, bias_col=None,
            banks=(4, 5, 6, 7), s_ready=(), bank_guard=()):
    b_s1, b_s2, b_m, b_q = [ln.pb[b] for b in banks]
    for pch in range(PCH):
        buf = pch % 2
        xc = ln.xc[:, buf * TC:(buf + 1) * TC]
        sq = ln.sq[:, buf * TC:(buf + 1) * TC]
        src = chunk_ap(s_sb, pch)
        w = list(s_ready) + [(cT, ln.pe_free[buf])]
        vv = sched.opi("vector", lambda v, xc=xc, src=src: v.tensor_copy(xc, src),
                       cV, waits=w)
        sv = sched.opi("scalar", lambda s, sq=sq, src=src: s.activation(sq, src, AF.Square),
                       cS, waits=w)
        st, sp = (pch == 0), (pch == PCH - 1)
        wb = ([(cV, vv), (cV, ln.row_free)] + list(bank_guard)) if pch == 0 \
            else [(cV, vv)]
        sched.op("tensor",
                 lambda t, xc=xc, st=st, sp=sp: t.matmul(b_s1[0:1, :], ln.ones_col[:],
                                                         xc, start=st, stop=sp), wb)
        ln.pe_free[buf] = sched.opi(
            "tensor",
            lambda t, sq=sq, st=st, sp=sp: t.matmul(b_s2[0:1, :], ln.ones_col[:],
                                                    sq, start=st, stop=sp),
            cT, waits=[(cS, sv)])
    t_stats = cT.v
    sched.op("vector", lambda v: v.tensor_scalar(ln.rows[0:1, 0:TC], b_s1[0:1, :],
                                                 1.0 / D, None, ALU.mult),
             [(cT, t_stats)])
    ln.row_free = sched.opi(
        "vector", lambda v: v.tensor_scalar(ln.rows[0:1, TC:2 * TC], b_s2[0:1, :],
                                            1.0 / D, None, ALU.mult), cV)
    v_rows = ln.row_free
    sched.op("tensor", lambda t: t.matmul(b_m[:], ln.ones_row[:], ln.rows[0:1, 0:TC],
                                          start=True, stop=True),
             [(cV, v_rows), (cV, ln.uRC_free)])
    t_bc = sched.opi("tensor",
                     lambda t: t.matmul(b_q[:], ln.ones_row[:], ln.rows[0:1, TC:2 * TC],
                                        start=True, stop=True), cT)
    s_u = sched.opi("scalar", lambda s: s.activation(ln.u, b_m[:], AF.Square),
                    cS, waits=[(cT, t_bc)])
    v_var = sched.opi("vector",
                      lambda v: v.scalar_tensor_tensor(ln.var, ln.u, -1.0, b_q[:],
                                                       ALU.mult, ALU.add),
                      cV, waits=[(cS, s_u)])
    sched.op("scalar", lambda s: s.activation(ln.u, ln.var, AF.Ln, bias=eps_col),
             [(cV, v_var)])
    s_R = sched.opi("scalar", lambda s: s.activation(ln.R, ln.u, AF.Exp, scale=-0.5),
                    cS)
    v_C = sched.opi("vector",
                    lambda v: v.scalar_tensor_tensor(ln.C, b_m[:], -1.0, ln.R,
                                                     ALU.mult, ALU.mult),
                    cV, waits=[(cS, s_R)])
    sched.op("vector", lambda v: v.tensor_tensor(r3(s_sb), r3(s_sb), bcast8(ln.R),
                                                 ALU.mult), [(cV, v_C)])
    v_app = sched.opi("vector",
                      lambda v: v.tensor_tensor(r3(s_sb), r3(s_sb), bcast8(ln.C),
                                                ALU.add), cV)
    ln.uRC_free = v_app
    if scale_col is None:
        return ("v", v_app)
    last = None
    for pch in range(PCH):
        sc, bc = scale_col(pch), bias_col(pch)
        if pch < PCH - 1:
            sched.op("scalar",
                     lambda s, pch=pch, sc=sc, bc=bc: s.activation(
                         chunk_ap(s_sb, pch), chunk_ap(s_sb, pch), AF.Identity,
                         bias=bc, scale=sc), [(cV, v_app)])
        else:
            last = sched.opi("scalar",
                             lambda s, pch=pch, sc=sc, bc=bc: s.activation(
                                 chunk_ap(s_sb, pch), chunk_ap(s_sb, pch), AF.Identity,
                                 bias=bc, scale=sc), cS, waits=[(cV, v_app)])
    return ("s", last)


def new_nc(detect_races=False):
    return bass.Bass("TRN2", target_bir_lowering=False, debug=False,
                     detect_race_conditions=detect_races)


def feature_major_dram(x):
    return x.rearrange("(c p) t -> p c t", p=128)


def build_p0():
    nc = new_nc()
    x = nc.dram_tensor("x", [D, T], F32, kind="ExternalInput").ap()
    y = nc.dram_tensor("y", [D, T], F32, kind="ExternalOutput").ap()
    xr, yr = feature_major_dram(x), feature_major_dram(y)
    with ExitStack() as ctx:
        s_sb = ctx.enter_context(nc.sbuf_tensor("s_sb", [128, PCH * TC], F32))
        pb = [ctx.enter_context(nc.psum_tensor(f"pb{i}", [128, 512], F32))
              for i in range(8)]
        ln = LNState(nc, ctx, pb)
        cp = CPool(nc, ctx, [EPS_MOD])
        sems = {n: ctx.enter_context(nc.semaphore(n))
                for n in ("sLD", "sST", "cT", "cS", "cV")}
        cLD, cST = Ctr(sems["sLD"]), Ctr(sems["sST"])
        cT, cS, cV = Ctr(sems["cT"]), Ctr(sems["cS"]), Ctr(sems["cV"])
        block = ctx.enter_context(nc.Block())
        sched = Sched()
        ln.emit_init(sched)
        cp.emit_init(sched)
        for tc in range(NTC):
            w = [(cST, 16 * tc)]
            sched.opi("sync", lambda s, tc=tc: s.dma_start(
                out=s_sb.rearrange("p (c t) -> p c t", t=TC),
                in_=xr[:, :, tc * TC:(tc + 1) * TC]), cLD, n=16, waits=w)
            need = [(cLD, cLD.v)]
            eng, val = emit_ln(sched, ln, s_sb, cp.col(EPS_MOD), cT, cS, cV, s_ready=need)
            cEnd = {"v": cV, "s": cS}[eng]
            sched.opi("sync", lambda s, tc=tc: s.dma_start(
                out=yr[:, :, tc * TC:(tc + 1) * TC],
                in_=s_sb.rearrange("p (c t) -> p c t", t=TC)), cST, n=16,
                waits=[(cEnd, val)])
        sched.op("sync", lambda s: s.wait_ge(cST.sem, cST.v))
        sched.emit(block)
    return nc



VC = {
    "bq": 0, "bk": 8, "bv": 16, "bo": 24, "b1": 32, "b2": 64,
    "ln1_s": 72, "ln1_b": 80, "ln2_s": 88, "ln2_b": 96,
}
VCN = 104


class Env:
    pass


def emit_block_A(sched, E, dram):
    nc = E.nc
    cLD, cST, cG, cT, cS, cV = E.cLD, E.cST, E.cG, E.cT, E.cS, E.cV
    s_sb, xb, h_sb, scr, pb = E.s_sb, E.xb, E.h_sb, E.scr, E.pb
    vecc = E.vecc
    ln = E.ln

    def vcol(name, i):
        return vecc[:, VC[name] + i: VC[name] + i + 1]

    def scrs(i):
        return scr[:, i * TC:(i + 1) * TC]

    def vals(i):
        return h_sb[:, 8192 + i * 1024: 8192 + (i + 1) * 1024].bitcast(F32)

    s_in_r = feature_major_dram(dram["s_in"])
    s_mid_r = feature_major_dram(dram["s_mid"])
    s_out_r = feature_major_dram(dram["s_out"])

    Wk_sb = E.wpool[:, 0:8192]
    Wv_sb = E.wpool[:, 8192:16384]
    W1_sb = E.wpool[:, 0:32768]
    Wq_sb = E.wpool[:, 32768:40960]
    Wo_sb = E.wpool[:, 40960:49152]
    W2_sb = E.wpool[:, 32768:65536]
    V_sb = h_sb[:, 0:4096]

    def wtile(sb, kc, oc, kspan=1024):
        return sb[:, kc * kspan + oc * 128: kc * kspan + oc * 128 + 128]

    def ldw(name, dst_view, hbm, shape_kc):
        src = hbm.rearrange("(kc p) o -> p kc o", p=128)
        dst = dst_view.rearrange("p (kc o) -> p kc o", kc=shape_kc)
        pre = cG.v

        def build(g):
            if pre:
                g.wait_ge(cG.sem, pre)
            return g.dma_start(out=dst, in_=src)
        return sched.opi("gpsimd", build, cG, n=16, waits=E.wloadw)

    E.wloadw = list(E.wpool_guard)
    ldw("Wk", Wk_sb, dram["Wk"], 8)
    g_kv = ldw("Wv", Wv_sb, dram["Wv"], 8)
    E.wloadw = list(E.w2region_guard)
    ldw("Wq", Wq_sb, dram["Wq"], 8)
    g_qkvo = ldw("Wo", Wo_sb, dram["Wo"], 8)
    E.wloadw = []

    banks = E.bankfree

    kv_fin = {}
    for tc in range(NTC):
        seq, tcs = tc // 4, tc % 4
        sched.opi("sync", lambda s, tc=tc: s.dma_start(
            out=s_sb.rearrange("p (c t) -> p c t", t=TC),
            in_=s_in_r[:, :, tc * TC:(tc + 1) * TC]), cLD, n=16,
            waits=[E.s_free])
        ld_v = cLD.v
        xb_v = sched.opi("vector", lambda v: v.tensor_copy(xb[:], chunk_ap(s_sb)),
                         cV, waits=[(cLD, ld_v), E.xb_free])
        E.s_free = (cV, xb_v)
        last_mm = None
        for oc in range(8):
            pk = oc % 2
            pv = 2 + oc % 2
            e_s, r_s, val_s = scrs(oc % 2), scrs(2 + oc % 2), vals(oc % 2)
            mmv_k = None
            for kc in range(8):
                wts = [(cV, xb_v), (cG, g_kv), banks[pk]] if kc == 0 else []
                bld = lambda t, kc=kc, oc=oc, pk=pk: t.matmul(
                    pb[pk][:], wtile(Wk_sb, kc, oc), xb[:, kc * TC:(kc + 1) * TC],
                    start=(kc == 0), stop=(kc == 7))
                if kc == 7:
                    mmv_k = sched.opi("tensor", bld, cT, waits=wts)
                else:
                    sched.op("tensor", bld, wts)
            mmv_v = None
            for kc in range(8):
                wts = [banks[pv]] if kc == 0 else []
                bld = lambda t, kc=kc, oc=oc, pv=pv: t.matmul(
                    pb[pv][:], wtile(Wv_sb, kc, oc), xb[:, kc * TC:(kc + 1) * TC],
                    start=(kc == 0), stop=(kc == 7))
                if kc == 7:
                    mmv_v = sched.opi("tensor", bld, cT, waits=wts)
                else:
                    sched.op("tensor", bld, wts)
            last_mm = mmv_v
            scw = [(cT, mmv_k), E.scr_free[oc % 2]]
            sched.op("scalar", lambda s, e_s=e_s, oc=oc, pk=pk: s.activation(
                e_s, pb[pk][:], AF.Exp, bias=vcol("bk", oc)), scw)
            sched.op("scalar", lambda s, r_s=r_s, oc=oc, pk=pk: s.activation(
                r_s, pb[pk][:], AF.Relu, bias=vcol("bk", oc)))
            sv = sched.opi("scalar", lambda s, val_s=val_s, oc=oc, pv=pv: s.activation(
                val_s, pb[pv][:], AF.Identity, bias=vcol("bv", oc)), cS,
                waits=[(cT, mmv_v)])
            banks[pk] = (cS, sv)
            banks[pv] = (cS, sv)
            sched.op("vector", lambda v, e_s=e_s, r_s=r_s: v.scalar_tensor_tensor(
                e_s, e_s, 1.0, r_s, ALU.min, ALU.add), [(cS, sv)])
            sched.op("vector", lambda v, e_s=e_s, val_s=val_s: v.tensor_tensor(
                val_s, e_s, val_s, ALU.mult))
            ci = (seq * 8 + oc) * 4 + tcs
            sched.op("vector", lambda v, val_s=val_s, ci=ci: v.tensor_reduce(
                E.kvparts[:, ci:ci + 1], val_s, AX.X, ALU.add))
            vv = sched.opi("vector", lambda v, e_s=e_s, ci=ci: v.tensor_reduce(
                E.ksparts[:, ci:ci + 1], e_s, AX.X, ALU.add), cV)
            E.scr_free[oc % 2] = (cV, vv)
        E.xb_free = (cT, last_mm)
        if tcs == 3:
            kvi = E.kv[:, seq * 8: seq * 8 + 8].rearrange("p (g e) -> p g e", e=1)
            ksi = E.ks[:, seq * 8: seq * 8 + 8].rearrange("p (g e) -> p g e", e=1)
            kvp = E.kvparts[:, seq * 32: seq * 32 + 32].rearrange("p (g e) -> p g e", e=4)
            ksp = E.ksparts[:, seq * 32: seq * 32 + 32].rearrange("p (g e) -> p g e", e=4)
            sched.op("vector", lambda v, kvi=kvi, kvp=kvp: v.tensor_reduce(
                kvi, kvp, AX.X, ALU.add))
            kv_fin[seq] = sched.opi("vector", lambda v, ksi=ksi, ksp=ksp: v.tensor_reduce(
                ksi, ksp, AX.X, ALU.add), cV)
    sweep1_mm_done = last_mm

    E.wloadw = [(cT, sweep1_mm_done)]
    g_w1 = ldw("W1", W1_sb, dram["W1"], 8)
    E.wloadw = []

    for tc in range(NTC):
        seq = tc // 4
        sched.opi("sync", lambda s, tc=tc: s.dma_start(
            out=s_sb.rearrange("p (c t) -> p c t", t=TC),
            in_=s_in_r[:, :, tc * TC:(tc + 1) * TC]), cLD, n=16,
            waits=[E.s_free, (cST, cST.v)])
        ld_v = cLD.v
        xb_v = sched.opi("vector", lambda v: v.tensor_copy(xb[:], chunk_ap(s_sb)),
                         cV, waits=[(cLD, ld_v), E.xb_free])
        last_mm = None
        v_allV = None
        for oc in range(8):
            pk = oc % 2
            e_s, r_s, val_s = scrs(oc % 2), scrs(2 + oc % 2), vals(oc % 2)
            mmv = None
            for kc in range(8):
                wts = [(cV, xb_v), (cG, g_qkvo), banks[pk]] if kc == 0 else []
                bld = lambda t, kc=kc, oc=oc, pk=pk: t.matmul(
                    pb[pk][:], wtile(Wq_sb, kc, oc), xb[:, kc * TC:(kc + 1) * TC],
                    start=(kc == 0), stop=(kc == 7))
                if kc == 7:
                    mmv = sched.opi("tensor", bld, cT, waits=wts)
                else:
                    sched.op("tensor", bld, wts)
            scw = [(cT, mmv), E.scr_free[oc % 2]]
            sched.op("scalar", lambda s, e_s=e_s, oc=oc, pk=pk: s.activation(
                e_s, pb[pk][:], AF.Exp, bias=vcol("bq", oc)), scw)
            sv = sched.opi("scalar", lambda s, r_s=r_s, oc=oc, pk=pk: s.activation(
                r_s, pb[pk][:], AF.Relu, bias=vcol("bq", oc)), cS)
            banks[pk] = (cS, sv)
            qv = sched.opi("vector", lambda v, e_s=e_s, r_s=r_s: v.scalar_tensor_tensor(
                e_s, e_s, 1.0, r_s, ALU.min, ALU.add), cV,
                waits=[(cS, sv), (cV, kv_fin[seq])])
            ksc = E.ks[:, seq * 8 + oc: seq * 8 + oc + 1]
            kvc = E.kv[:, seq * 8 + oc: seq * 8 + oc + 1]
            t1v = sched.opi("vector", lambda v, r_s=r_s, e_s=e_s, ksc=ksc: v.tensor_scalar(
                r_s, e_s, ksc, EPS_ATTN, ALU.mult, ALU.add), cV)
            sched.op("scalar", lambda s, val_s=val_s, r_s=r_s: s.activation(
                val_s, r_s, AF.Ln), [(cV, t1v)])
            zv = sched.opi("scalar", lambda s, val_s=val_s: s.activation(
                val_s, val_s, AF.Exp, scale=-1.0), cS)
            sched.op("vector", lambda v, e_s=e_s, kvc=kvc: v.tensor_scalar(
                e_s, e_s, kvc, None, ALU.mult), [(cS, zv), E.V_free])
            v_allV = sched.opi("vector", lambda v, oc=oc, e_s=e_s, val_s=val_s:
                               v.tensor_tensor(V_sb[:, oc * TC:(oc + 1) * TC],
                                               e_s, val_s, ALU.mult), cV)
            E.scr_free[oc % 2] = (cV, v_allV)
        for oc2 in range(8):
            pa = 2 + oc2 % 2
            g_s = scrs(oc2 % 2)
            mmv = None
            for oc in range(8):
                wts = [(cV, v_allV), banks[pa]] if oc == 0 else []
                bld = lambda t, oc=oc, oc2=oc2, pa=pa: t.matmul(
                    pb[pa][:], wtile(Wo_sb, oc, oc2),
                    V_sb[:, oc * TC:(oc + 1) * TC],
                    start=(oc == 0), stop=(oc == 7))
                if oc == 7:
                    mmv = sched.opi("tensor", bld, cT, waits=wts)
                else:
                    sched.op("tensor", bld, wts)
            sv = sched.opi("scalar", lambda s, g_s=g_s, oc2=oc2, pa=pa: s.activation(
                g_s, pb[pa][:], AF.Gelu_apprx_tanh, bias=vcol("bo", oc2)),
                cS, waits=[(cT, mmv), E.scr_free[oc2 % 2]])
            banks[pa] = (cS, sv)
            rv = sched.opi("vector", lambda v, g_s=g_s, oc2=oc2: v.tensor_tensor(
                chunk_ap(s_sb, oc2), chunk_ap(s_sb, oc2), g_s, ALU.add),
                cV, waits=[(cS, sv)])
            E.scr_free[oc2 % 2] = (cV, rv)
        E.V_free = (cT, cT.v)
        E.xb_free = (cT, cT.v)
        eng, lnv = emit_ln(sched, ln, s_sb, E.eps_ln_col, cT, cS, cV,
                           scale_col=lambda p: vcol("ln1_s", p),
                           bias_col=lambda p: vcol("ln1_b", p),
                           s_ready=[(cV, rv)])
        cEnd = {"v": cV, "s": cS}[eng]
        sched.opi("sync", lambda s, tc=tc: s.dma_start(
            out=s_mid_r[:, :, tc * TC:(tc + 1) * TC],
            in_=s_sb.rearrange("p (c t) -> p c t", t=TC)), cST, n=16,
            waits=[(cEnd, lnv)])
        E.s_free = (cST, cST.v)
    sweep2_mm_done = cT.v

    E.wloadw = [(cT, sweep2_mm_done)]
    g_w2 = ldw("W2", W2_sb, dram["W2"], 32)
    E.wloadw = []

    for tc in range(NTC):
        sched.opi("sync", lambda s, tc=tc: s.dma_start(
            out=s_sb.rearrange("p (c t) -> p c t", t=TC),
            in_=s_mid_r[:, :, tc * TC:(tc + 1) * TC]), cLD, n=16,
            waits=[E.s_free, (cST, cST.v)])
        ld_v = cLD.v
        xb_v = sched.opi("vector", lambda v: v.tensor_copy(xb[:], chunk_ap(s_sb)),
                         cV, waits=[(cLD, ld_v), E.xb_free])
        h_all = None
        for oc in range(32):
            ph = oc % 3
            mmv = None
            for kc in range(8):
                wts = [(cV, xb_v), (cG, g_w1), banks[ph], E.h_free] if kc == 0 else []
                bld = lambda t, kc=kc, oc=oc, ph=ph: t.matmul(
                    pb[ph][:], W1_sb[:, kc * 4096 + oc * 128: kc * 4096 + oc * 128 + 128],
                    xb[:, kc * TC:(kc + 1) * TC],
                    start=(kc == 0), stop=(kc == 7))
                if kc == 7:
                    mmv = sched.opi("tensor", bld, cT, waits=wts)
                else:
                    sched.op("tensor", bld, wts)
            h_all = sched.opi("scalar", lambda s, oc=oc, ph=ph: s.activation(
                h_sb[:, oc * TC:(oc + 1) * TC], pb[ph][:], AF.Gelu_apprx_tanh,
                bias=vcol("b1", oc)), cS, waits=[(cT, mmv)])
            banks[ph] = (cS, h_all)
        E.xb_free = (cT, cT.v)
        rv = None
        for oc2 in range(8):
            pa = oc2 % 2
            g_s = scrs(oc2 % 2)
            mmv = None
            for kc2 in range(32):
                wts = [(cS, h_all), (cG, g_w2), banks[pa]] if kc2 == 0 else []
                bld = lambda t, kc2=kc2, oc2=oc2, pa=pa: t.matmul(
                    pb[pa][:], W2_sb[:, kc2 * 1024 + oc2 * 128: kc2 * 1024 + oc2 * 128 + 128],
                    h_sb[:, kc2 * TC:(kc2 + 1) * TC],
                    start=(kc2 == 0), stop=(kc2 == 31))
                if kc2 == 31:
                    mmv = sched.opi("tensor", bld, cT, waits=wts)
                else:
                    sched.op("tensor", bld, wts)
            sv = sched.opi("scalar", lambda s, g_s=g_s, oc2=oc2, pa=pa: s.activation(
                g_s, pb[pa][:], AF.Gelu_apprx_tanh, bias=vcol("b2", oc2)),
                cS, waits=[(cT, mmv), E.scr_free[oc2 % 2]])
            banks[pa] = (cS, sv)
            rv = sched.opi("vector", lambda v, g_s=g_s, oc2=oc2: v.tensor_tensor(
                chunk_ap(s_sb, oc2), chunk_ap(s_sb, oc2), g_s, ALU.add),
                cV, waits=[(cS, sv)])
            E.scr_free[oc2 % 2] = (cV, rv)
        E.h_free = (cT, cT.v)
        eng, lnv = emit_ln(sched, ln, s_sb, E.eps_ln_col, cT, cS, cV,
                           scale_col=lambda p: vcol("ln2_s", p),
                           bias_col=lambda p: vcol("ln2_b", p),
                           s_ready=[(cV, rv)])
        cEnd = {"v": cV, "s": cS}[eng]
        sched.opi("sync", lambda s, tc=tc: s.dma_start(
            out=s_out_r[:, :, tc * TC:(tc + 1) * TC],
            in_=s_sb.rearrange("p (c t) -> p c t", t=TC)), cST, n=16,
            waits=[(cEnd, lnv)])
        E.s_free = (cST, cST.v)
    E.wpool_guard = [(cT, cT.v)]
    E.w2region_guard = [(cT, cT.v)]


def make_env(nc, ctx):
    E = Env()
    E.nc = nc
    E.s_sb = ctx.enter_context(nc.sbuf_tensor("s_sb", [128, PCH * TC], F32))
    E.xb = ctx.enter_context(nc.sbuf_tensor("xb", [128, PCH * TC], BF16))
    E.wpool = ctx.enter_context(nc.sbuf_tensor("wpool", [128, 65536], BF16))
    E.h_sb = ctx.enter_context(nc.sbuf_tensor("h_sb", [128, 32 * TC], BF16))
    E.scr = ctx.enter_context(nc.sbuf_tensor("scr", [128, 8 * TC], F32))
    E.kvparts = ctx.enter_context(nc.sbuf_tensor("kvparts", [128, 64], F32))
    E.ksparts = ctx.enter_context(nc.sbuf_tensor("ksparts", [128, 64], F32))
    E.kv = ctx.enter_context(nc.sbuf_tensor("kv", [128, 16], F32))
    E.ks = ctx.enter_context(nc.sbuf_tensor("ks", [128, 16], F32))
    E.vecc = ctx.enter_context(nc.sbuf_tensor("vecc_sb", [128, VCN], F32))
    E.pb = [ctx.enter_context(nc.psum_tensor(f"pb{i}", [128, 512], F32))
            for i in range(8)]
    E.ln = LNState(nc, ctx, E.pb, scr4=E.scr[:, 4 * TC:8 * TC])
    cpv = [EPS_MOD, EPS_LN]
    E.cp = CPool(nc, ctx, cpv)
    names = ("sLD", "sST", "sG", "cT", "cS", "cV")
    E.sems = {n: ctx.enter_context(nc.semaphore(n)) for n in names}
    E.cLD = Ctr(E.sems["sLD"])
    E.cST = Ctr(E.sems["sST"])
    E.cG = Ctr(E.sems["sG"])
    E.cT = Ctr(E.sems["cT"])
    E.cS = Ctr(E.sems["cS"])
    E.cV = Ctr(E.sems["cV"])
    E.bankfree = {i: (E.cT, 0) for i in range(8)}
    E.scr_free = {0: (E.cV, 0), 1: (E.cV, 0)}
    E.s_free = (E.cST, 0)
    E.xb_free = (E.cT, 0)
    E.V_free = (E.cT, 0)
    E.h_free = (E.cT, 0)
    E.wpool_guard = []
    E.w2region_guard = []
    E.eps_ln_col = E.cp.col(EPS_LN)
    E.eps_mod_col = E.cp.col(EPS_MOD)
    return E


def build_A():
    nc = new_nc()
    dram = {}
    dram["s_in"] = nc.dram_tensor("s_in", [D, T], F32, kind="ExternalInput").ap()
    dram["s_mid"] = nc.dram_tensor("s_mid", [D, T], F32, kind="Internal").ap()
    dram["s_out"] = nc.dram_tensor("s_out", [D, T], F32, kind="ExternalOutput").ap()
    for w, shp in (("Wq", [D, D]), ("Wk", [D, D]), ("Wv", [D, D]), ("Wo", [D, D]),
                   ("W1", [D, FF]), ("W2", [FF, D])):
        dram[w] = nc.dram_tensor(w, shp, F32, kind="ExternalInput").ap()
    vecc_d = nc.dram_tensor("vecc", [128, VCN], F32, kind="ExternalInput").ap()
    with ExitStack() as ctx:
        E = make_env(nc, ctx)
        block = ctx.enter_context(nc.Block())
        sched = Sched()
        E.ln.emit_init(sched)
        E.cp.emit_init(sched)
        vl = sched.opi("sync", lambda s: s.dma_start(out=E.vecc[:], in_=vecc_d),
                       E.cLD, n=16)
        sched.op("sync", lambda s, vl=vl: s.wait_ge(E.cLD.sem, vl))
        for e in ("scalar", "vector"):
            sched.op(e, lambda x: None, [(E.cLD, vl)])
        E.blk = 0
        emit_block_A(sched, E, dram)
        sched.op("sync", lambda s: s.wait_ge(E.cST.sem, E.cST.v))
        sched.emit(block)
    return nc


def pack_vecc(d, i):
    cols = np.zeros((128, VCN), np.float32)

    def put(name, vec):
        n = vec.shape[0] // 128
        cols[:, VC[name]:VC[name] + n] = vec.reshape(n, 128).T
    put("bq", d["L0_attn_bq"][i]); put("bk", d["L0_attn_bk"][i])
    put("bv", d["L0_attn_bv"][i]); put("bo", d["L0_attn_bo"][i])
    put("b1", d["L0_attn_b1"][i]); put("b2", d["L0_attn_b2"][i])
    put("ln1_s", d["L0_attn_ln1_s"][i]); put("ln1_b", d["L0_attn_ln1_b"][i])
    put("ln2_s", d["L0_attn_ln2_s"][i]); put("ln2_b", d["L0_attn_ln2_b"][i])
    return cols



def build_B():
    nc = new_nc()
    x0 = nc.dram_tensor("x0", [D, T], F32, kind="ExternalInput").ap()
    s7 = nc.dram_tensor("s7", [D, T], F32, kind="ExternalInput").ap()
    x2 = nc.dram_tensor("x2", [D, T], F32, kind="ExternalOutput").ap()
    WrT = nc.dram_tensor("WrT", [D, D], F32, kind="ExternalInput").ap()
    WkT = nc.dram_tensor("WkT", [D, D], F32, kind="ExternalInput").ap()
    WvT = nc.dram_tensor("WvT", [D, D], F32, kind="ExternalInput").ap()
    muv_d = nc.dram_tensor("muv", [128, 16], F32, kind="ExternalInput").ap()
    x0r, s7r, x2r = (feature_major_dram(a) for a in (x0, s7, x2))
    with ExitStack() as ctx:
        s_sb = ctx.enter_context(nc.sbuf_tensor("s_sb", [128, PCH * TC], F32))
        t_sb = ctx.enter_context(nc.sbuf_tensor("t_sb", [128, PCH * TC], F32))
        y_sb = ctx.enter_context(nc.sbuf_tensor("y_sb", [128, PCH * TC], F32))
        r_sb = ctx.enter_context(nc.sbuf_tensor("r_sb", [128, PCH * TC], F32))
        xr_b = ctx.enter_context(nc.sbuf_tensor("xr_b", [128, PCH * TC], BF16))
        xk_b = ctx.enter_context(nc.sbuf_tensor("xk_b", [128, PCH * TC], BF16))
        kact = ctx.enter_context(nc.sbuf_tensor("kact", [128, PCH * TC], BF16))
        scr = ctx.enter_context(nc.sbuf_tensor("scr", [128, 8 * TC], F32))
        wpool = ctx.enter_context(nc.sbuf_tensor("wpool", [128, 3 * 8192], BF16))
        muv = ctx.enter_context(nc.sbuf_tensor("muv_sb", [128, 16], F32))
        omu = ctx.enter_context(nc.sbuf_tensor("omu_sb", [128, 16], F32))
        ybnd = ctx.enter_context(nc.sbuf_tensor("ybnd", [128, 8], F32))
        pb = [ctx.enter_context(nc.psum_tensor(f"pb{i}", [128, 512], F32))
              for i in range(8)]
        ln = LNState(nc, ctx, pb, scr4=scr[:, 4 * TC:8 * TC])
        cp = CPool(nc, ctx, [EPS_MOD])
        sems = {n: ctx.enter_context(nc.semaphore(n))
                for n in ("sLD", "sST", "sG", "cT", "cS", "cV")}
        cLD, cST, cG = Ctr(sems["sLD"]), Ctr(sems["sST"]), Ctr(sems["sG"])
        cT, cS, cV = Ctr(sems["cT"]), Ctr(sems["cS"]), Ctr(sems["cV"])
        block = ctx.enter_context(nc.Block())
        sched = Sched()
        ln.emit_init(sched)
        cp.emit_init(sched)
        Wr_sb = wpool[:, 0:8192]
        Wk_sb = wpool[:, 8192:16384]
        Wv_sb = wpool[:, 16384:24576]

        def wtile(sb, kc, oc):
            return sb[:, kc * 1024 + oc * 128: kc * 1024 + oc * 128 + 128]

        g_w = 0
        for w_sb, hbm in ((Wr_sb, WrT), (Wk_sb, WkT), (Wv_sb, WvT)):
            pre = cG.v

            def build(g, w_sb=w_sb, hbm=hbm, pre=pre):
                if pre:
                    g.wait_ge(cG.sem, pre)
                return g.dma_start(
                    out=w_sb.rearrange("p (kc o) -> p kc o", kc=8),
                    in_=hbm.rearrange("(kc p) o -> p kc o", p=128))
            g_w = sched.opi("gpsimd", build, cG, n=16)
        sched.opi("sync", lambda s: s.dma_start(out=muv[:], in_=muv_d), cLD, n=16)
        sched.op("sync", lambda s, v=cLD.v: s.wait_ge(cLD.sem, v))
        v_omu = sched.opi("vector", lambda v: v.tensor_scalar(
            omu[:], muv[:], -1.0, 1.0, ALU.mult, ALU.add), cV,
            waits=[(cLD, cLD.v)])
        banks = {i: (cT, 0) for i in range(8)}
        s_free = (cST, 0)
        rv = 0
        for tc in range(NTC):
            sched.opi("sync", lambda s, tc=tc: s.dma_start(
                out=s_sb.rearrange("p (c t) -> p c t", t=TC),
                in_=x0r[:, :, tc * TC:(tc + 1) * TC]), cLD, n=16, waits=[s_free])
            sched.op("sync", lambda s, v=cLD.v: s.wait_ge(cLD.sem, v))
            sched.opi("sync", lambda s, tc=tc: s.dma_start(
                out=t_sb.rearrange("p (c t) -> p c t", t=TC),
                in_=s7r[:, :, tc * TC:(tc + 1) * TC]), cLD, n=16)
            ld_v = cLD.v
            sched.op("vector", lambda v: v.tensor_tensor(
                chunk_ap(s_sb), chunk_ap(s_sb), chunk_ap(t_sb), ALU.add),
                [(cLD, ld_v), (cV, rv)])
            v_x1 = sched.opi("vector", lambda v: v.tensor_copy(
                chunk_ap(y_sb), chunk_ap(s_sb)), cV)
            eng, lnv = emit_ln(sched, ln, y_sb, cp.col(EPS_MOD), cT, cS, cV,
                               s_ready=[(cV, v_x1)])
            if tc % 4 == 0:
                sched.op("vector", lambda v: v.memset(ybnd[:], 0.0), [(cV, lnv)])
            for pch in range(PCH):
                a = scr[:, 0:TC]
                for (xmix, mu_i, omu_i) in ((xr_b, 0, 0), (xk_b, 8, 8)):
                    mu_c = muv[:, mu_i + pch: mu_i + pch + 1]
                    omu_c = omu[:, omu_i + pch: omu_i + pch + 1]
                    sched.op("vector", lambda v, pch=pch, mu_c=mu_c: v.tensor_scalar(
                        scr[:, 0:TC], chunk_ap(y_sb, pch), mu_c, None, ALU.mult),
                        [(cV, lnv) if eng == "v" else (cS, lnv)])
                    sched.op("vector", lambda v, pch=pch, omu_c=omu_c, xmix=xmix:
                             v.scalar_tensor_tensor(
                                 xmix[:, pch * TC + 1: (pch + 1) * TC],
                                 y_sb[:, pch * TC: (pch + 1) * TC - 1], omu_c,
                                 scr[:, 1:TC], ALU.mult, ALU.add))
                    sched.op("vector", lambda v, pch=pch, omu_c=omu_c, xmix=xmix:
                             v.scalar_tensor_tensor(
                                 xmix[:, pch * TC: pch * TC + 1],
                                 ybnd[:, pch:pch + 1], omu_c,
                                 scr[:, 0:1], ALU.mult, ALU.add))
            v_mix = sched.opi("vector", lambda v: v.tensor_copy(
                ybnd[:], y_sb[:, TC - 1::TC]), cV)
            for (xmix, w_sb, pk0, dst, actf) in (
                    (xr_b, Wr_sb, 0, "r", AF.Sigmoid),
                    (xk_b, Wk_sb, 2, "k", None)):
                for oc in range(8):
                    pk = pk0 + oc % 2
                    mmv = None
                    for kc in range(8):
                        wts = [(cV, v_mix), (cG, g_w), banks[pk]] if kc == 0 else []
                        bld = lambda t, kc=kc, oc=oc, pk=pk, w_sb=w_sb, xmix=xmix:                             t.matmul(pb[pk][:], wtile(w_sb, kc, oc),
                                     xmix[:, kc * TC:(kc + 1) * TC],
                                     start=(kc == 0), stop=(kc == 7))
                        if kc == 7:
                            mmv = sched.opi("tensor", bld, cT, waits=wts)
                        else:
                            sched.op("tensor", bld, wts)
                    if dst == "r":
                        sv = sched.opi("scalar", lambda s, oc=oc, pk=pk: s.activation(
                            r_sb[:, oc * TC:(oc + 1) * TC], pb[pk][:], AF.Sigmoid),
                            cS, waits=[(cT, mmv)])
                    else:
                        sched.op("scalar", lambda s, oc=oc, pk=pk: s.activation(
                            scr[:, TC:2 * TC], pb[pk][:], AF.Relu),
                            [(cT, mmv)])
                        sv = sched.opi("scalar", lambda s, oc=oc: s.activation(
                            kact[:, oc * TC:(oc + 1) * TC], scr[:, TC:2 * TC],
                            AF.Square), cS)
                    banks[pk] = (cS, sv)
            s_kact = cS.v
            for oc in range(8):
                pk = 4 + oc % 2 if False else oc % 2
                mmv = None
                for kc in range(8):
                    wts = [(cS, s_kact), banks[pk]] if kc == 0 else []
                    bld = lambda t, kc=kc, oc=oc, pk=pk: t.matmul(
                        pb[pk][:], wtile(Wv_sb, kc, oc),
                        kact[:, kc * TC:(kc + 1) * TC],
                        start=(kc == 0), stop=(kc == 7))
                    if kc == 7:
                        mmv = sched.opi("tensor", bld, cT, waits=wts)
                    else:
                        sched.op("tensor", bld, wts)
                vv = sched.opi("vector", lambda v, oc=oc, pk=pk: v.tensor_tensor(
                    scr[:, 0:TC], r_sb[:, oc * TC:(oc + 1) * TC], pb[pk][:],
                    ALU.mult), cV, waits=[(cT, mmv)])
                rv = sched.opi("vector", lambda v, oc=oc: v.tensor_tensor(
                    chunk_ap(s_sb, oc), chunk_ap(s_sb, oc), scr[:, 0:TC],
                    ALU.add), cV)
                banks[pk] = (cV, rv)
            sched.opi("sync", lambda s, tc=tc: s.dma_start(
                out=x2r[:, :, tc * TC:(tc + 1) * TC],
                in_=s_sb.rearrange("p (c t) -> p c t", t=TC)), cST, n=16,
                waits=[(cV, rv)])
            s_free = (cST, cST.v)
        sched.op("sync", lambda s: s.wait_ge(cST.sem, cST.v))
        sched.emit(block)
    return nc


def pack_muv(d):
    cols = np.zeros((128, 16), np.float32)
    cols[:, 0:8] = d["L0_ffn_mu_r"].reshape(8, 128).T
    cols[:, 8:16] = d["L0_ffn_mu_k"].reshape(8, 128).T
    return cols



def build_C():
    nc = new_nc()
    x2 = nc.dram_tensor("x2", [D, T], F32, kind="ExternalInput").ap()
    x3 = nc.dram_tensor("x3", [D, T], F32, kind="ExternalOutput").ap()
    WqT = nc.dram_tensor("WqT", [D, D], F32, kind="ExternalInput").ap()
    WkT = nc.dram_tensor("WkT", [D, D], F32, kind="ExternalInput").ap()
    WvT = nc.dram_tensor("WvT", [D, D], F32, kind="ExternalInput").ap()
    WoT = nc.dram_tensor("WoT", [D, D], F32, kind="ExternalInput").ap()
    x2r, x3r = feature_major_dram(x2), feature_major_dram(x3)
    SCALE = 0.125
    EPS1 = 1e-6
    with ExitStack() as ctx:
        s_sb = ctx.enter_context(nc.sbuf_tensor("s_sb", [128, PCH * TC], F32))
        yb = ctx.enter_context(nc.sbuf_tensor("yb", [128, PCH * TC], BF16))
        qT = ctx.enter_context(nc.sbuf_tensor("qT", [128, PCH * 2048], BF16))
        kT = ctx.enter_context(nc.sbuf_tensor("kT", [128, PCH * 2048], BF16))
        v_sb = ctx.enter_context(nc.sbuf_tensor("v_sb", [128, 16 * 1040], BF16))
        out_sb = ctx.enter_context(nc.sbuf_tensor("out_sb", [128, PCH * TC], BF16))
        wT = ctx.enter_context(nc.sbuf_tensor("wT", [128, 1024], BF16))
        rz = ctx.enter_context(nc.sbuf_tensor("rz", [128, TC], F32))
        stage = ctx.enter_context(nc.sbuf_tensor("stage", [128, TC], BF16))
        zrow = ctx.enter_context(nc.sbuf_tensor("zrow", [128, TC], BF16))
        ones128 = ctx.enter_context(nc.sbuf_tensor("ones128", [128, 128], BF16))
        scr = ctx.enter_context(nc.sbuf_tensor("scr", [128, 4 * TC], F32))
        wpool = ctx.enter_context(nc.sbuf_tensor("wpool", [128, 3 * 8192], BF16))
        pb = [ctx.enter_context(nc.psum_tensor(f"pb{i}", [128, 512], F32))
              for i in range(6)]
        pbs = ctx.enter_context(nc.psum_tensor("pbs", [128, 1024], F32))
        ln = LNState(nc, ctx, pb, scr4=scr)
        cp = CPool(nc, ctx, [EPS_MOD, EPS1])
        sems = {n: ctx.enter_context(nc.semaphore(n))
                for n in ("sLD", "sST", "sG", "sSH", "cT", "cS", "cV")}
        cLD, cST, cG = Ctr(sems["sLD"]), Ctr(sems["sST"]), Ctr(sems["sG"])
        cSH = Ctr(sems["sSH"])
        cT, cS, cV = Ctr(sems["cT"]), Ctr(sems["cS"]), Ctr(sems["cV"])
        block = ctx.enter_context(nc.Block())
        sched = Sched()
        ln.emit_init(sched)
        cp.emit_init(sched)
        sched.op("vector", lambda v: v.memset(ones128[:], 1.0))

        W0 = wpool[:, 0:8192]
        W1s = wpool[:, 8192:16384]
        W2s = wpool[:, 16384:24576]

        def wtile(sb, kc, oc):
            return sb[:, kc * 1024 + oc * 128: kc * 1024 + oc * 128 + 128]

        def ldw(dst, hbm, guard):
            pre = cG.v

            def build(g, dst=dst, hbm=hbm, pre=pre):
                if pre:
                    g.wait_ge(cG.sem, pre)
                return g.dma_start(
                    out=dst.rearrange("p (kc o) -> p kc o", kc=8),
                    in_=hbm.rearrange("(kc p) o -> p kc o", p=128))
            return sched.opi("gpsimd", build, cG, n=16, waits=guard)

        banks = {i: (cT, 0) for i in range(6)}
        bank_s = (cT, 0)
        s_free = (cST, 0)
        stage_free = (cSH, 0)
        E_wo_guard = (cT, 0)
        wT_free = (cT, 0)
        rz_free = (cV, 0)
        v_guard = (cT, 0)
        w0_guard = []
        for seq in range(2):
            ldw(W0, WqT, w0_guard)
            ldw(W1s, WkT, [])
            g_qkv = ldw(W2s, WvT, [])
            chunk_guard = [banks[i] for i in range(6)]
            for tcs in range(4):
                tc = seq * 4 + tcs
                sched.opi("sync", lambda s, tc=tc: s.dma_start(
                    out=s_sb.rearrange("p (c t) -> p c t", t=TC),
                    in_=x2r[:, :, tc * TC:(tc + 1) * TC]), cLD, n=16,
                    waits=[s_free])
                ld_v = cLD.v
                eng, lnv = emit_ln(sched, ln, s_sb, cp.col(EPS_MOD), cT, cS, cV,
                                   s_ready=[(cLD, ld_v)],
                                   banks=(0, 1, 2, 3), bank_guard=chunk_guard)
                yv = sched.opi("vector", lambda v: v.tensor_copy(yb[:], chunk_ap(s_sb)),
                               cV, waits=[(cV, lnv)])
                s_free = (cV, yv)
                for (wsb, dstT, pk0) in ((W0, qT, 0), (W1s, kT, 2)):
                    for oc in range(8):
                        pk = pk0 + oc % 2
                        mmv = None
                        for kc in range(8):
                            wts = [(cV, yv), (cG, g_qkv), banks[pk]] if kc == 0 else []
                            bld = lambda t, kc=kc, oc=oc, pk=pk, wsb=wsb: t.matmul(
                                pb[pk][:], wtile(wsb, kc, oc),
                                yb[:, kc * TC:(kc + 1) * TC],
                                start=(kc == 0), stop=(kc == 7))
                            if kc == 7:
                                mmv = sched.opi("tensor", bld, cT, waits=wts)
                            else:
                                sched.op("tensor", bld, wts)
                        dst = dstT[:, oc * 2048 + tcs * TC: oc * 2048 + (tcs + 1) * TC]
                        ev = sched.opi("vector", lambda v, dst=dst, pk=pk:
                                       v.tensor_copy(dst, pb[pk][:]), cV,
                                       waits=[(cT, mmv)])
                        banks[pk] = (cV, ev)
                if tcs == 0:
                    mv = sched.opi("vector", lambda v: v.memset(v_sb[:], 1.0), cV,
                                   waits=[v_guard])
                for tt in range(4):
                    kt = tcs * 4 + tt
                    for on in range(2):
                        pk = 4 + on
                        mmv = None
                        for kc in range(8):
                            wts = [banks[pk]] if kc == 0 else []
                            bld = lambda t, kc=kc, tt=tt, on=on, pk=pk: t.matmul(
                                pb[pk][:],
                                yb[:, kc * TC + tt * 128: kc * TC + tt * 128 + 128],
                                W2s[:, kc * 1024 + on * 512: kc * 1024 + on * 512 + 512],
                                start=(kc == 0), stop=(kc == 7))
                            if kc == 7:
                                mmv = sched.opi("tensor", bld, cT, waits=wts)
                            else:
                                sched.op("tensor", bld, wts)
                        dst = v_sb[:, kt * 1040 + on * 520: kt * 1040 + (on + 1) * 520] \
                            .rearrange("p (h i) -> p h i", i=65)[:, :, 0:64]
                        src = pb[pk][:].rearrange("p (h i) -> p h i", i=64)
                        ev = sched.opi("vector", lambda v, dst=dst, src=src:
                                       v.tensor_copy(dst, src), cV,
                                       waits=[(cT, mmv), (cV, mv)])
                        banks[pk] = (cV, ev)
                chunk_guard = [banks[i] for i in range(6)]
            c1_done = (cV, cV.v)
            g_wo = ldw(W0, WoT, [(cT, cT.v)])
            for qt in range(4):
                tc = seq * 4 + qt
                sched.opi("sync", lambda s, tc=tc: s.dma_start(
                    out=s_sb.rearrange("p (c t) -> p c t", t=TC),
                    in_=x2r[:, :, tc * TC:(tc + 1) * TC]), cLD, n=16,
                    waits=[s_free])
                ld_v = cLD.v
                ev_last = (cV, 0)
                for head in range(16):
                    hp = (head % 2) * 64
                    hc = head // 2
                    av_mm = None
                    for ktp in range(8):
                        sc_mm = None
                        for half in range(2):
                            kt = ktp * 2 + half
                            wts = [bank_s, c1_done] if half == 0 else []
                            bld = lambda t, kt=kt, half=half, hp=hp, hc=hc, qt=qt: \
                                t.matmul(
                                    pbs[:, half * TC:(half + 1) * TC],
                                    kT[hp:hp + 64,
                                       hc * 2048 + kt * 128: hc * 2048 + kt * 128 + 128],
                                    qT[hp:hp + 64,
                                       hc * 2048 + qt * TC: hc * 2048 + (qt + 1) * TC],
                                    start=True, stop=True, skip_group_check=True)
                            if half == 1:
                                sc_mm = sched.opi("tensor", bld, cT, waits=wts)
                            else:
                                sched.op("tensor", bld, wts)
                        sg = sched.opi("scalar", lambda s: s.activation(
                            wT[:], pbs[:], AF.Sigmoid, scale=SCALE), cS,
                            waits=[(cT, sc_mm), wT_free])
                        bank_s = (cS, sg)
                        for half in range(2):
                            kt = ktp * 2 + half
                            wts = [(cS, sg)] + ([banks[2]] if kt == 0 else [])
                            bld = lambda t, kt=kt, half=half, head=head: t.matmul(
                                pb[2][0:65, :],
                                v_sb[:, kt * 1040 + head * 65: kt * 1040 + head * 65 + 65],
                                wT[:, half * TC:(half + 1) * TC],
                                start=(kt == 0), stop=(kt == 15),
                                skip_group_check=True)
                            if half == 1:
                                av_mm = sched.opi("tensor", bld, cT, waits=wts)
                                wT_free = (cT, av_mm)
                            else:
                                sched.op("tensor", bld, wts)
                    zc = sched.opi("vector", lambda v: v.tensor_copy(
                        zrow[64:65, :], pb[2][64:65, :]), cV, waits=[(cT, av_mm)])
                    bc = sched.opi("tensor", lambda t: t.matmul(
                        pb[3][:], ones128[64:65, :], zrow[64:65, :],
                        start=True, stop=True), cT,
                        waits=[(cV, zc), banks[3]])
                    sched.op("scalar", lambda s: s.activation(
                        rz[:], pb[3][:], AF.Ln, bias=cp.col(EPS1)),
                        [(cT, bc), rz_free])
                    rzv = sched.opi("scalar", lambda s: s.activation(
                        rz[:], rz[:], AF.Exp, scale=-1.0), cS)
                    banks[3] = (cS, rzv)
                    if hp == 0:
                        dst = out_sb[0:64, hc * TC:(hc + 1) * TC]
                        ev = sched.opi("vector", lambda v, dst=dst: v.tensor_tensor(
                            dst, pb[2][0:64, :], rz[0:64, :], ALU.mult), cV,
                            waits=[(cS, rzv), E_wo_guard])
                        ev_last = (cV, ev)
                        banks[2] = (cV, ev)
                        rz_free = (cV, ev)
                    else:
                        ev = sched.opi("vector", lambda v: v.tensor_tensor(
                            stage[0:64, :], pb[2][0:64, :], rz[0:64, :], ALU.mult),
                            cV, waits=[(cS, rzv), stage_free])
                        banks[2] = (cV, ev)
                        rz_free = (cV, ev)
                        pre = cSH.v

                        def build_sh(g, hc=hc, pre=pre):
                            if pre:
                                g.wait_ge(cSH.sem, pre)
                            return g.dma_start(
                                out=out_sb[64:128, hc * TC:(hc + 1) * TC],
                                in_=stage[0:64, :])
                        sched.opi("gpsimd", build_sh, cSH, n=16,
                                  waits=[(cV, ev), E_wo_guard])
                        stage_free = (cSH, cSH.v)
                sh_all = (cSH, cSH.v)
                for oc2 in range(8):
                    pk = 4 + oc2 % 2
                    mmv = None
                    for hc2 in range(8):
                        wts = [ev_last, sh_all, (cG, g_wo), banks[pk]] \
                            if hc2 == 0 else []
                        bld = lambda t, hc2=hc2, oc2=oc2, pk=pk: t.matmul(
                            pb[pk][:], wtile(W0, hc2, oc2),
                            out_sb[:, hc2 * TC:(hc2 + 1) * TC],
                            start=(hc2 == 0), stop=(hc2 == 7))
                        if hc2 == 7:
                            mmv = sched.opi("tensor", bld, cT, waits=wts)
                        else:
                            sched.op("tensor", bld, wts)
                    rv = sched.opi("vector", lambda v, oc2=oc2, pk=pk: v.tensor_tensor(
                        chunk_ap(s_sb, oc2), chunk_ap(s_sb, oc2), pb[pk][:],
                        ALU.add), cV, waits=[(cT, mmv), (cLD, ld_v)])
                    banks[pk] = (cV, rv)
                E_wo_guard = (cT, cT.v)
                v_guard = (cT, cT.v)
                sched.opi("sync", lambda s, tc=tc: s.dma_start(
                    out=x3r[:, :, tc * TC:(tc + 1) * TC],
                    in_=s_sb.rearrange("p (c t) -> p c t", t=TC)), cST, n=16,
                    waits=[(cV, rv)])
                s_free = (cST, cST.v)
            w0_guard = [(cT, cT.v)]
        sched.op("sync", lambda s: s.wait_ge(cST.sem, cST.v))
        sched.emit(block)
    return nc



GRID_MIDS = (0.25, 0.75, 1.25, 1.75, 2.5, 3.5, 5.0)
QENG = "vector"
DBAR = False


class QuantState:
    def __init__(self, qt0, qt1, qt2, qt3, sc_sb, rsc_sb, cS, cV, cP):
        self.qt = (qt0, qt1, qt2, qt3)
        self.sc_sb, self.rsc_sb = sc_sb, rsc_sb
        self.cS, self.cV, self.cP = cS, cV, cP
        self.tmp_free = {0: (cP, 0), 1: (cV, 0), 2: (cV, 0), 3: (cV, 0)}


def emit_quant_q(sched, qs, src, dst, W, src_ready):
    cS, cV, cP = qs.cS, qs.cV, qs.cP
    tmp_free = qs.tmp_free
    nb = W // 16
    t0, t1, t2, t3 = (q[:, 0:W] for q in qs.qt)
    sc = qs.sc_sb[:, 0:nb]
    rsc = qs.rsc_sb[:, 0:nb]
    w3 = lambda ap: ap.rearrange("p (w e) -> p w e", e=16)
    scb = lambda ap: ap.rearrange("p (w e) -> p w e", e=1) \
        .broadcast_to([128, nb, 16])
    sched.op("vector", lambda v: v.tensor_reduce(
        sc.rearrange("p (w e) -> p w e", e=1), w3(src), AX.X, ALU.max,
        apply_absolute_value=True), list(src_ready) + [tmp_free[1]])
    vr = sched.opi("vector", lambda v: v.tensor_scalar(
        sc, sc, 1.0 / 6.0, 1e-12, ALU.mult, ALU.max), cV)
    sched.op("scalar", lambda s: s.activation(rsc, sc, AF.Ln), [(cV, vr)])
    sr = sched.opi("scalar", lambda s: s.activation(rsc, rsc, AF.Exp,
                                                    scale=-1.0), cS)
    sa = sched.opi("scalar", lambda s: s.activation(t1, src, AF.Abs),
                   cS, waits=list(src_ready) + [tmp_free[1]])
    vu = sched.opi("vector", lambda v: v.tensor_tensor(
        w3(t1), w3(t1), scb(rsc), ALU.mult), cV, waits=[(cS, sa)])
    sched.op(QENG, lambda g: g.tensor_single_scalar(
        t0, t1, 0.25, ALU.is_gt), [(cV, vu), tmp_free[0]])
    for m in (0.75, 1.25):
        sched.op(QENG, lambda g, m=m: g.scalar_tensor_tensor(
            t0, t1, m, t0, ALU.is_gt, ALU.add))
    pc = sched.opi(QENG, lambda g: g.scalar_tensor_tensor(
        t0, t1, 1.75, t0, ALU.is_gt, ALU.add), cP)
    sched.op("vector", lambda v: v.tensor_single_scalar(
        t2, t1, 2.5, ALU.is_gt), [tmp_free[2]])
    sched.op("vector", lambda v: v.scalar_tensor_tensor(
        t2, t1, 3.5, t2, ALU.is_gt, ALU.add))
    vee = sched.opi("vector", lambda v: v.tensor_single_scalar(
        t3, t1, 5.0, ALU.is_gt), cV, waits=[tmp_free[3]])
    sched.op("vector", lambda v: v.scalar_tensor_tensor(
        t2, t0, 0.5, t2, ALU.mult, ALU.add), [(cP, pc)])
    vq = sched.opi("vector", lambda v: v.scalar_tensor_tensor(
        t3, t3, 2.0, t2, ALU.mult, ALU.add), cV)
    tmp_free[0] = (cV, vq)
    tmp_free[2] = (cV, vq)
    sg = sched.opi("scalar", lambda s: s.activation(t1, src, AF.Sign),
                   cS, waits=[(cV, vq), (cP, pc)])
    sched.op("vector", lambda v: v.tensor_tensor(
        w3(t1), w3(t1), scb(sc), ALU.mult), [(cS, sg)])
    vd = sched.opi("vector", lambda v: v.tensor_tensor(
        dst, t3, t1, ALU.mult), cV)
    tmp_free[1] = (cV, vd)
    tmp_free[3] = (cV, vd)
    return (cV, vd)


def build_D(dbg=False):
    nc = new_nc()
    I = 2816
    NKC = 22
    x3 = nc.dram_tensor("x3", [D, T], F32, kind="ExternalInput").ap()
    x4 = nc.dram_tensor("x4", [D, T], F32, kind="ExternalOutput").ap()
    wg_d = nc.dram_tensor("wg", [I, D], F32, kind="ExternalInput").ap()
    wu_d = nc.dram_tensor("wu", [I, D], F32, kind="ExternalInput").ap()
    wd_d = nc.dram_tensor("wd", [D, I], F32, kind="ExternalInput").ap()
    eye_f_d = nc.dram_tensor("eye_f", [128, 128], F32, kind="ExternalInput").ap()
    eye_b_d = nc.dram_tensor("eye_b", [128, 128], BF16, kind="ExternalInput").ap()
    wd_scr = nc.dram_tensor("wd_scr", [128, 8 * I], BF16, kind="Internal").ap()
    if dbg:
        dbg_wgq = nc.dram_tensor("dbg_wgq", [128, 8 * I], BF16,
                                 kind="ExternalOutput").ap()
        dbg_xtm = nc.dram_tensor("dbg_xtm", [128, 1024], F32,
                                 kind="ExternalOutput").ap()
        dbg_xqT = nc.dram_tensor("dbg_xqT", [128, PCH * TC], BF16,
                                 kind="ExternalOutput").ap()
        dbg_hqT = nc.dram_tensor("dbg_hqT", [128, NKC * TC], BF16,
                                 kind="ExternalOutput").ap()
    x3r, x4r = feature_major_dram(x3), feature_major_dram(x4)
    with ExitStack() as ctx:
        s_sb = ctx.enter_context(nc.sbuf_tensor("s_sb", [128, PCH * TC], F32))
        wgq = ctx.enter_context(nc.sbuf_tensor("wgq", [128, 8 * I], BF16))
        wuq = ctx.enter_context(nc.sbuf_tensor("wuq", [128, 8 * I], BF16))
        wdsl = ctx.enter_context(nc.sbuf_tensor("wdsl", [128, 2 * I], BF16))
        xtm = ctx.enter_context(nc.sbuf_tensor("xtm", [128, 1024], F32))
        hi_b = ctx.enter_context(nc.sbuf_tensor("hi_b", [128, 1024], BF16))
        lo_b = ctx.enter_context(nc.sbuf_tensor("lo_b", [128, 1024], BF16))
        xq_tm = ctx.enter_context(nc.sbuf_tensor("xq_tm", [128, 1024], BF16))
        xqT = ctx.enter_context(nc.sbuf_tensor("xqT", [128, PCH * TC], BF16))
        htm = ctx.enter_context(nc.sbuf_tensor("htm", [128, 2 * I], F32))
        hq_tm = ctx.enter_context(nc.sbuf_tensor("hq_tm", [128, I], BF16))
        hqT = ctx.enter_context(nc.sbuf_tensor("hqT", [128, NKC * TC], BF16))
        qt0 = ctx.enter_context(nc.sbuf_tensor("qt0", [128, 1024], F32))
        qt1 = ctx.enter_context(nc.sbuf_tensor("qt1", [128, 1024], F32))
        qt2 = ctx.enter_context(nc.sbuf_tensor("qt2", [128, 1024], F32))
        qt3 = ctx.enter_context(nc.sbuf_tensor("qt3", [128, 1024], F32))
        sc_sb = ctx.enter_context(nc.sbuf_tensor("sc_sb", [128, 176], F32))
        rsc_sb = ctx.enter_context(nc.sbuf_tensor("rsc_sb", [128, 176], F32))
        col_sb = ctx.enter_context(nc.sbuf_tensor("col_sb", [128, 8], F32))
        eye_f = ctx.enter_context(nc.sbuf_tensor("eye_f_sb", [128, 128], F32))
        eye_b = ctx.enter_context(nc.sbuf_tensor("eye_b_sb", [128, 128], BF16))
        scr = ctx.enter_context(nc.sbuf_tensor("scr", [128, 2 * TC], F32))
        wsl = htm[:, 0:I]
        wq_sl = hqT[:, 0:I]
        wstg = hqT[:, I:2 * I]
        cp = CPool(nc, ctx, [EPS_MOD])
        pb = [ctx.enter_context(nc.psum_tensor(f"pb{i}", [128, 512], F32))
              for i in range(4)]
        pbt = [ctx.enter_context(nc.psum_tensor(f"pbt{i}", [128, 1024], BF16))
               for i in range(2)]
        sems = {n: ctx.enter_context(nc.semaphore(n))
                for n in ("sLD", "sST", "sG", "cT", "cS", "cV", "cP")}
        cLD, cST, cG = Ctr(sems["sLD"]), Ctr(sems["sST"]), Ctr(sems["sG"])
        cT, cS, cV, cP = (Ctr(sems[n]) for n in ("cT", "cS", "cV", "cP"))
        block = ctx.enter_context(nc.Block())
        sched = Sched()
        cp.emit_init(sched)
        sched.opi("sync", lambda s: s.dma_start(out=eye_f[:], in_=eye_f_d),
                  cLD, n=16)
        sched.op("sync", lambda s, v=cLD.v: s.wait_ge(cLD.sem, v))
        sched.opi("sync", lambda s: s.dma_start(out=eye_b[:], in_=eye_b_d),
                  cLD, n=16)
        sched.op("sync", lambda s, v=cLD.v: s.wait_ge(cLD.sem, v))
        eyes_v = cLD.v

        qs = QuantState(qt0, qt1, qt2, qt3, sc_sb, rsc_sb, cS, cV, cP)

        def emit_quant(src, dst, W, src_ready):
            return emit_quant_q(sched, qs, src, dst, W, src_ready)

        banks = {i: (cT, 0) for i in range(4)}
        bankt = {i: (cV, 0) for i in range(2)}

        def transpose_pack(tiles, dtype_eye, pbx, bank_guard, ready):
            mm = None
            for i, (in_ap, _) in enumerate(tiles):
                wts = list(ready) + [bank_guard] if i == 0 else []
                bld = lambda t, in_ap=in_ap, i=i: t.transpose(
                    pbx[:, i * 128:(i + 1) * 128], in_ap, dtype_eye)
                if i == len(tiles) - 1:
                    mm = sched.opi("tensor", bld, cT, waits=wts)
                else:
                    sched.op("tensor", bld, wts)
            evs = None
            for i, (_, out_ap) in enumerate(tiles):
                evs = sched.opi("vector", lambda v, out_ap=out_ap, i=i:
                                v.tensor_copy(out_ap, pbx[:, i * 128:(i + 1) * 128]),
                                cV, waits=[(cT, mm)])
            return evs

        wq_free = (cT, 0)
        for (hbm, dstq) in ((wg_d, wgq), (wu_d, wuq)):
            for sl in range(NKC):
                pre = cLD.v
                sched.opi("sync", lambda s, hbm=hbm, sl=sl, pre=pre:
                          (s.wait_ge(cLD.sem, pre),
                           s.dma_start(out=wsl[:, 0:1024],
                                       in_=hbm[sl * 128:(sl + 1) * 128, :]))[1],
                          cLD, n=16, waits=[wq_free])
                qv = emit_quant(wsl[:, 0:1024], wq_sl[:, 0:1024], 1024,
                                [(cLD, cLD.v), wq_free])
                tiles = [(wq_sl[:, kc * 128:(kc + 1) * 128],
                          dstq[:, kc * I + sl * 128: kc * I + sl * 128 + 128])
                         for kc in range(8)]
                ev = transpose_pack(tiles, eye_b[:], pbt[sl % 2][:, 0:1024],
                                    bankt[sl % 2], [qv, (cLD, eyes_v)])
                bankt[sl % 2] = (cV, ev)
                wq_free = (cV, ev)
        for sl in range(8):
            pre = cLD.v
            sched.opi("sync", lambda s, sl=sl, pre=pre:
                      (s.wait_ge(cLD.sem, pre),
                       s.dma_start(out=wsl[:], in_=wd_d[sl * 128:(sl + 1) * 128, :]))[1],
                      cLD, n=16, waits=[wq_free])
            ldv = cLD.v
            evs = []
            for (o0, w) in ((0, 1024), (1024, 1024), (2048, 768)):
                qv = emit_quant(wsl[:, o0:o0 + w], wq_sl[:, o0:o0 + w], w,
                                [(cLD, ldv), wq_free])
                wq_free = qv
            for grp in range(3):
                kcs = range(grp * 8, min((grp + 1) * 8, NKC))
                tiles = [(wq_sl[:, kc * 128:(kc + 1) * 128],
                          wstg[:, kc * 128:(kc + 1) * 128]) for kc in kcs]
                ev = transpose_pack(tiles, eye_b[:], pbt[grp % 2][:, 0:128 * len(tiles)],
                                    bankt[grp % 2], [wq_free])
                bankt[grp % 2] = (cV, ev)
                evs.append(ev)
            pre = cG.v
            sched.opi("gpsimd", lambda g, sl=sl, pre=pre, ev=evs[-1]:
                      (g.wait_ge(cG.sem, pre) if pre else None,
                       g.dma_start(out=wd_scr[:, sl * I:(sl + 1) * I],
                                   in_=wstg[:]))[1], cG, n=16,
                      waits=[(cV, evs[-1])])
            wq_free = (cG, cG.v)
        wq_all = (cG, cG.v)
        if dbg:
            sched.opi("sync", lambda s: s.dma_start(out=dbg_wgq, in_=wgq[:]),
                      cST, n=16, waits=[wq_free, (cV, cV.v)])

        sched.op("sync", lambda s, v=cLD.v: s.wait_ge(cLD.sem, v))
        s_free = (cST, 0)
        xq_free = (cT, 0)
        xqtm_free = (cV, 0)
        hilo_free = (cT, 0)
        silu_scr_free = {0: (cV, 0), 1: (cV, 0)}
        hq_free = wq_all
        htm_free = {0: wq_all, 1: wq_all}
        wd_ld = (cG, 0)
        for tc in range(NTC):
            if DBAR:
                soft_barrier(sched, (cT, cS, cV, cG, cP, cLD, cST))
            sched.opi("sync", lambda s, tc=tc: s.dma_start(
                out=s_sb.rearrange("p (c t) -> p c t", t=TC),
                in_=x3r[:, :, tc * TC:(tc + 1) * TC]), cLD, n=16,
                waits=[s_free])
            ld_v = cLD.v
            for tt in range(4):
                s3d = s_sb.rearrange("p (c t) -> p c t", t=TC)[:, :, tt * 128:
                                                              (tt + 1) * 128]
                w0 = [(cLD, ld_v), (cLD, eyes_v), hilo_free,
                      silu_scr_free[0], silu_scr_free[1]]
                sched.op("vector", lambda v, s3d=s3d: v.tensor_copy(
                    hi_b.rearrange("p (c t) -> p c t", t=128), s3d), w0)
                hl = sched.opi("vector", lambda v, s3d=s3d: v.tensor_tensor(
                    lo_b.rearrange("p (c t) -> p c t", t=128), s3d,
                    hi_b.rearrange("p (c t) -> p c t", t=128), ALU.subtract), cV)
                mm = None
                for i in range(2 * PCH):
                    src = (hi_b if i < PCH else lo_b)[:, (i % PCH) * 128:
                                                      (i % PCH + 1) * 128]
                    pbx = pbt[i // PCH]
                    wts = ([(cV, hl), bankt[0], bankt[1]] if i == 0 else [])
                    bld = lambda t, src=src, i=i, pbx=pbx: t.transpose(
                        pbx[:, (i % PCH) * 128:(i % PCH + 1) * 128], src, eye_b[:])
                    if i == 2 * PCH - 1:
                        mm = sched.opi("tensor", bld, cT, waits=wts)
                    else:
                        sched.op("tensor", bld, wts)
                sched.op("vector", lambda v: v.tensor_copy(xtm[:], pbt[0][:]),
                         [(cT, mm), qs.tmp_free[1]])
                ev = sched.opi("vector", lambda v: v.tensor_tensor(
                    xtm[:], xtm[:], pbt[1][:], ALU.add), cV)
                bankt[0] = (cV, ev)
                bankt[1] = (cV, ev)
                hilo_free = (cT, mm)
                sched.op("vector", lambda v: v.tensor_reduce(
                    col_sb[:, 0:1], xtm[:], AX.X, ALU.add), [(cV, ev)])
                sq = sched.opi("scalar", lambda s: s.activation(
                    scr[:, 0:1024], xtm[:], AF.Square), cS, waits=[(cV, ev)])
                sched.op("vector", lambda v: v.tensor_reduce(
                    col_sb[:, 1:2], scr[:, 0:1024], AX.X, ALU.add), [(cS, sq)])
                sched.op("vector", lambda v: v.tensor_scalar(
                    col_sb[:, 0:1], col_sb[:, 0:1], 1.0 / D, None, ALU.mult))
                sched.op("vector", lambda v: v.tensor_scalar(
                    col_sb[:, 1:2], col_sb[:, 1:2], 1.0 / D, None, ALU.mult))
                sched.op("vector", lambda v: v.tensor_tensor(
                    col_sb[:, 2:3], col_sb[:, 0:1], col_sb[:, 0:1], ALU.mult))
                vv = sched.opi("vector", lambda v: v.tensor_tensor(
                    col_sb[:, 1:2], col_sb[:, 1:2], col_sb[:, 2:3],
                    ALU.subtract), cV)
                sched.op("scalar", lambda s: s.activation(
                    col_sb[:, 1:2], col_sb[:, 1:2], AF.Ln, bias=cp.col(EPS_MOD)),
                    [(cV, vv)])
                rs = sched.opi("scalar", lambda s: s.activation(
                    col_sb[:, 1:2], col_sb[:, 1:2], AF.Exp, scale=-0.5), cS)
                lnv = sched.opi("vector", lambda v: v.tensor_scalar(
                    xtm[:], xtm[:], col_sb[:, 0:1], col_sb[:, 1:2],
                    ALU.subtract, ALU.mult), cV, waits=[(cS, rs)])
                dumped = dbg and tc == 0 and tt == 0
                if dumped:
                    drn = sched.opi("vector", lambda v: v.tensor_copy(
                        col_sb[:, 3:4], col_sb[:, 2:3]), cV)
                    sched.opi("sync", lambda s: s.dma_start(out=dbg_xtm,
                                                            in_=xtm[:]),
                              cST, n=16, waits=[(cV, drn)])
                    dump_v = (cST, cST.v)
                qv = emit_quant(xtm[:], xq_tm[:], 1024,
                                [(cV, lnv), xqtm_free])
                if dumped:
                    qs.tmp_free[1] = dump_v
                tiles = [(xq_tm[:, pch * 128:(pch + 1) * 128],
                          xqT[:, pch * TC + tt * 128: pch * TC + tt * 128 + 128])
                         for pch in range(PCH)]
                ev2 = transpose_pack(tiles, eye_b[:], pbt[tt % 2][:, 0:1024],
                                     bankt[tt % 2], [qv, xq_free])
                bankt[tt % 2] = (cV, ev2)
                xqtm_free = (cV, ev2)
            xq_done = (cV, cV.v)
            if dbg and tc == 0:
                sched.opi("sync", lambda s: s.dma_start(out=dbg_xqT, in_=xqT[:]),
                          cST, n=16, waits=[xq_done])
            for tt in range(4):
                hb = htm[:, (tt % 2) * I:(tt % 2) * I + I]
                for ncn in range(6):
                    n0 = ncn * 512
                    nw = min(512, I - n0)
                    mmg = mmu = None
                    for kc in range(PCH):
                        wts = ([xq_done, wq_all, banks[0], htm_free[tt % 2]]
                               if kc == 0 else [])
                        bld = lambda t, kc=kc, tt=tt, n0=n0, nw=nw: t.matmul(
                            pb[0][:, 0:nw],
                            xqT[:, kc * TC + tt * 128: kc * TC + tt * 128 + 128],
                            wgq[:, kc * I + n0: kc * I + n0 + nw],
                            start=(kc == 0), stop=(kc == 7))
                        if kc == 7:
                            mmg = sched.opi("tensor", bld, cT, waits=wts)
                        else:
                            sched.op("tensor", bld, wts)
                    for kc in range(PCH):
                        wts = [banks[1]] if kc == 0 else []
                        bld = lambda t, kc=kc, tt=tt, n0=n0, nw=nw: t.matmul(
                            pb[1][:, 0:nw],
                            xqT[:, kc * TC + tt * 128: kc * TC + tt * 128 + 128],
                            wuq[:, kc * I + n0: kc * I + n0 + nw],
                            start=(kc == 0), stop=(kc == 7))
                        if kc == 7:
                            mmu = sched.opi("tensor", bld, cT, waits=wts)
                        else:
                            sched.op("tensor", bld, wts)
                    sslot = (hi_b if ncn % 2 == 0 else lo_b)[:, 0:1024] \
                        .bitcast(F32)
                    sg = sched.opi("scalar", lambda s, nw=nw, sslot=sslot:
                                   s.activation(sslot[:, 0:nw],
                                                pb[0][:, 0:nw], AF.Silu), cS,
                                   waits=[(cT, mmg), silu_scr_free[ncn % 2]])
                    banks[0] = (cS, sg)
                    hv = sched.opi("vector", lambda v, hb=hb, n0=n0, nw=nw,
                                   sslot=sslot:
                                   v.tensor_tensor(hb[:, n0:n0 + nw],
                                                   sslot[:, 0:nw],
                                                   pb[1][:, 0:nw],
                                                   ALU.mult), cV,
                                   waits=[(cS, sg), (cT, mmu)])
                    banks[1] = (cV, hv)
                    silu_scr_free[ncn % 2] = (cV, hv)
                h_ready = [(cV, cV.v)]
                for (o0, w) in ((0, 1024), (1024, 1024), (2048, 768)):
                    qv = emit_quant(hb[:, o0:o0 + w], hq_tm[:, o0:o0 + w], w,
                                    h_ready)
                hdone = qv
                htm_free[tt % 2] = hdone
                for grp in range(3):
                    kcs = list(range(grp * 8, min((grp + 1) * 8, NKC)))
                    tiles = [(hq_tm[:, kc * 128:(kc + 1) * 128],
                              hqT[:, kc * TC + tt * 128: kc * TC + tt * 128 + 128])
                             for kc in kcs]
                    ev = transpose_pack(tiles, eye_b[:],
                                        pbt[grp % 2][:, 0:128 * len(tiles)],
                                        bankt[grp % 2], [hdone, hq_free])
                    bankt[grp % 2] = (cV, ev)
            hq_done = (cV, cV.v)
            if dbg and tc == 0:
                sched.opi("sync", lambda s: s.dma_start(out=dbg_hqT, in_=hqT[:]),
                          cST, n=16, waits=[hq_done])
            rv = None
            for oc3 in range(8):
                pre = cG.v

                def bldw(g, oc3=oc3, pre=pre):
                    if pre:
                        g.wait_ge(cG.sem, pre)
                    return g.dma_start(
                                out=wdsl[:, (oc3 % 2) * I:(oc3 % 2) * I + I],
                        in_=wd_scr[:, oc3 * I:(oc3 + 1) * I])
                sched.opi("gpsimd", bldw, cG, n=16, waits=[wd_ld])
                wd_v = cG.v
                pk = oc3 % 2
                mmv = None
                for kc in range(NKC):
                    wts = [hq_done, (cG, wd_v), banks[pk]] if kc == 0 else []
                    bld = lambda t, kc=kc, oc3=oc3, pk=pk: t.matmul(
                        pb[pk][:],
                        wdsl[:, (oc3 % 2) * I + kc * 128:(oc3 % 2) * I + kc * 128 + 128],
                        hqT[:, kc * TC:(kc + 1) * TC],
                        start=(kc == 0), stop=(kc == NKC - 1))
                    if kc == NKC - 1:
                        mmv = sched.opi("tensor", bld, cT, waits=wts)
                    else:
                        sched.op("tensor", bld, wts)
                wd_ld = (cT, mmv)
                rv = sched.opi("vector", lambda v, oc3=oc3, pk=pk: v.tensor_tensor(
                    chunk_ap(s_sb, oc3), chunk_ap(s_sb, oc3), pb[pk][:],
                    ALU.add), cV, waits=[(cT, mmv), (cLD, ld_v)])
                banks[pk] = (cV, rv)
            xq_free = (cT, cT.v)
            hq_free = (cT, cT.v)
            sched.opi("sync", lambda s, tc=tc: s.dma_start(
                out=x4r[:, :, tc * TC:(tc + 1) * TC],
                in_=s_sb.rearrange("p (c t) -> p c t", t=TC)), cST, n=16,
                waits=[(cV, rv)])
            s_free = (cST, cST.v)
        sched.op("sync", lambda s: s.wait_ge(cST.sem, cST.v))
        sched.emit(block)
    return nc


def shard_x(x_full):
    shards = []
    for c in range(NCORES):
        s = x_full[2 * c:2 * c + 2].reshape(T, D)
        shards.append(np.ascontiguousarray(s.T.astype(np.float32, copy=False)))
    return shards


def unshard_x(shards):
    out = np.empty((16, 2048, D), np.float32)
    for c in range(NCORES):
        out[2 * c:2 * c + 2] = np.asarray(shards[c], np.float32).T.reshape(2, 2048, D)
    return out


_PROGS = {}


def _prog(name, builder):
    if name not in _PROGS:
        _PROGS[name] = builder()
    return _PROGS[name]


def kernel(**inputs):
    d = inputs
    xs = shard_x(np.asarray(d["L0_attn_x"], np.float32))
    cores = list(range(NCORES))
    tr = lambda a: np.ascontiguousarray(np.asarray(a, np.float32).T)

    nc = _prog("p0", build_p0)
    res = run_bass_kernel_spmd(nc, [{"x": x} for x in xs], core_ids=cores)
    cur = [r["y"] for r in res.results]

    nc = _prog("A", build_A)
    for blk in range(NB):
        vecc = pack_vecc(d, blk)
        in_maps = [{
            "s_in": cur[c],
            "Wq": np.asarray(d["L0_attn_Wq"][blk], np.float32),
            "Wk": np.asarray(d["L0_attn_Wk"][blk], np.float32),
            "Wv": np.asarray(d["L0_attn_Wv"][blk], np.float32),
            "Wo": np.asarray(d["L0_attn_Wo"][blk], np.float32),
            "W1": np.asarray(d["L0_attn_W1"][blk], np.float32),
            "W2": np.asarray(d["L0_attn_W2"][blk], np.float32),
            "vecc": vecc,
        } for c in cores]
        res = run_bass_kernel_spmd(nc, in_maps, core_ids=cores)
        cur = [r["s_out"] for r in res.results]

    nc = _prog("B", build_B)
    muv = pack_muv(d)
    in_maps = [{
        "x0": xs[c], "s7": cur[c],
        "WrT": tr(d["L0_ffn_W_r"]), "WkT": tr(d["L0_ffn_W_k"]),
        "WvT": tr(d["L0_ffn_W_v"]), "muv": muv,
    } for c in cores]
    res = run_bass_kernel_spmd(nc, in_maps, core_ids=cores)
    cur = [r["x2"] for r in res.results]

    nc = _prog("C", build_C)
    in_maps = [{
        "x2": cur[c],
        "WqT": tr(d["L1_attn_Wq"]), "WkT": tr(d["L1_attn_Wk"]),
        "WvT": tr(d["L1_attn_Wv"]), "WoT": tr(d["L1_attn_Wo"]),
    } for c in cores]
    res = run_bass_kernel_spmd(nc, in_maps, core_ids=cores)
    cur = [r["x3"] for r in res.results]

    nc = _prog("D", build_D)
    eye_f = np.eye(128, dtype=np.float32)
    eye_b = np.eye(128, dtype=BF)
    in_maps = [{
        "x3": cur[c],
        "wg": np.asarray(d["L1_ffn_w_gate"], np.float32),
        "wu": np.asarray(d["L1_ffn_w_up"], np.float32),
        "wd": np.asarray(d["L1_ffn_w_down"], np.float32),
        "eye_f": eye_f, "eye_b": eye_b,
    } for c in cores]
    res = run_bass_kernel_spmd(nc, in_maps, core_ids=cores)
    return unshard_x([r["x4"] for r in res.results])
